# version 02 (final)
# baseline (speedup 1.0000x reference)
import numpy as np
import ml_dtypes
import concourse.bass as bass
import concourse.mybir as mybir
from concourse.bass_utils import run_bass_kernel_spmd

F32 = mybir.dt.float32
BF16 = mybir.dt.bfloat16
I32 = mybir.dt.int32
AF = mybir.ActivationFunctionType
ALU = mybir.AluOpType
AX = mybir.AxisListType
BF = ml_dtypes.bfloat16

B, S, E, H, D = 2, 8192, 2048, 16, 128
HID = 8192
TPC = 2048
NT = 16
NFC = 16
ENGINES = ("tensor", "vector", "scalar", "sync", "gpsimd")


class Sem:
    def __init__(self, h):
        self.h = h
        self.n = 0


class Chan:

    def __init__(self, mk, name, n=2):
        self.s = [mk(f"{name}{i}") for i in range(n)]

    def __getitem__(self, u):
        return self.s[u % len(self.s)]

    @property
    def total(self):
        return sum(x.n for x in self.s)


class Rec:
    __slots__ = ("attr", "args", "kwargs", "incs")

    def __init__(self, attr, args, kwargs):
        self.attr = attr
        self.args = args
        self.kwargs = kwargs
        self.incs = []

    def then_inc(self, sem, k=1):
        self.incs.append((sem, k))
        sem.n += k
        return self

    def run(self, eng):
        ins = getattr(eng, self.attr)(*self.args, **self.kwargs)
        for s, k in self.incs:
            ins.then_inc(s.h, k)


class VarRec:
    def __init__(self, name, sel_ap, streams_by_j):
        self.name = name
        self.sel_ap = sel_ap
        self.streams = streams_by_j

    def run(self, eng):
        with eng.register(self.name) as r:
            eng.reg_load(r, self.sel_ap)
            for j in range(4):
                with eng.If_eq(r, j):
                    for rec in self.streams[j]:
                        rec.run(eng)


class EngPlan:
    def __init__(self, plan, name):
        self._plan = plan
        self._name = name

    def op(self, attr, *args, **kwargs):
        r = Rec(attr, args, kwargs)
        self._plan.streams[self._name].append(r)
        return r

    def __getattr__(self, attr):
        if attr.startswith("_"):
            raise AttributeError(attr)
        return lambda *a, **kw: self.op(attr, *a, **kw)

    def wait(self, sem, val):
        if val > 0:
            self.op("wait_ge", sem.h, val)


class Plan:
    def __init__(self):
        self.streams = {e: [] for e in ENGINES}

    def flush(self, block):
        for name in ENGINES:
            recs = self.streams[name]
            if not recs:
                continue
            todo = list(recs)
            recs.clear()

            def body(eng, todo=todo):
                for r in todo:
                    r.run(eng)

            getattr(block, name)(body)


def qsb_limits(j):
    qs_list = [1024 * j, 1024 * j + 512, 1024 * (7 - j), 1024 * (7 - j) + 512]
    return qs_list, [(qs + 512) // 128 for qs in qs_list]


def build_nc(stage=4, dbg=False):
    import os
    nc = bass.Bass("TRN2", target_bir_lowering=False,
                   detect_race_conditions=not os.environ.get("MK_NORD"))
    ctx = nc.ctx

    x_in = nc.dram_tensor("x_own", [TPC, E], F32, kind="ExternalInput").ap()
    win = {}
    for nm, shp in (("wq", [E, E]), ("wk", [E, E]), ("wv", [E, E]), ("wo", [E, E]),
                    ("wg", [HID, E]), ("wu", [HID, E]), ("wd", [E, HID])):
        win[nm] = nc.dram_tensor(nm, shp, F32, kind="ExternalInput").ap()
    cosq_in = nc.dram_tensor("cosq", [128, NT * 64], F32, kind="ExternalInput").ap()
    sinq_in = nc.dram_tensor("sinq", [128, NT * 64], F32, kind="ExternalInput").ap()
    cosk_in = nc.dram_tensor("cosk", [128, NT * 64], F32, kind="ExternalInput").ap()
    sink_in = nc.dram_tensor("sink", [128, NT * 64], F32, kind="ExternalInput").ap()
    mask_in = nc.dram_tensor("mask", [128, 896], F32, kind="ExternalInput").ap()
    idbf_in = nc.dram_tensor("idbf", [128, 128], BF16, kind="ExternalInput").ap()
    idf_in = nc.dram_tensor("idf", [128, 128], F32, kind="ExternalInput").ap()
    sel_in = nc.dram_tensor("sel", [1, 1], I32, kind="ExternalInput").ap()
    out_t = nc.dram_tensor("out", [TPC, E], F32, kind="ExternalOutput")

    q_scr = nc.dram_tensor("q_scr", [TPC, E], BF16, kind="Internal")
    k_con = nc.dram_tensor("k_con", [TPC, E], BF16, kind="Internal")
    v_con = nc.dram_tensor("v_con", [TPC, E], BF16, kind="Internal")
    g_k = nc.dram_tensor("g_k", [4 * TPC, E], BF16, kind="Internal")
    g_v = nc.dram_tensor("g_v", [4 * TPC, E], BF16, kind="Internal")
    ao_scr = nc.dram_tensor("ao_scr", [TPC, E], BF16, kind="Internal")
    x2_scr = nc.dram_tensor("x2_scr", [TPC, E], F32, kind="Internal")
    wg_bf = nc.dram_tensor("wg_bf", [HID, E], BF16, kind="Internal")
    wu_bf = nc.dram_tensor("wu_bf", [HID, E], BF16, kind="Internal")
    wd_bf = nc.dram_tensor("wd_bf", [E, HID], BF16, kind="Internal")
    gat_scr = nc.dram_tensor("gat_scr", [HID, TPC], BF16, kind="Internal")
    ffo_scr = nc.dram_tensor("ffo_scr", [TPC, E], F32, kind="Internal")

    dbg_t = {}
    if dbg and stage == 1:
        for nm, shp, dt in (("d_kc", [TPC, E], BF16), ("d_vc", [TPC, E], BF16),
                            ("d_q", [TPC, E], BF16), ("d_n1", [128, NFC * TPC], BF16),
                            ("d_nt", [128, 2 * E], BF16)):
            dbg_t[nm] = nc.dram_tensor(nm, shp, dt, kind="ExternalOutput")
    if dbg and stage == 2:
        dbg_t["d_ao"] = nc.dram_tensor("d_ao", [TPC, E], BF16, kind="ExternalOutput")
    if dbg and stage == 3:
        dbg_t["d_x2"] = nc.dram_tensor("d_x2", [TPC, E], F32, kind="ExternalOutput")

    P = Plan()
    T = EngPlan(P, "tensor")
    V = EngPlan(P, "vector")
    A = EngPlan(P, "scalar")
    SY = EngPlan(P, "sync")
    G = EngPlan(P, "gpsimd")

    all_sems = []

    def sem(name):
        s = Sem(ctx.enter_context(nc.semaphore(name)))
        all_sems.append(s)
        return s

    d_c = sem("d_c");   d_cc = sem("d_cc")
    d_st = Chan(sem, "d_st")
    d_ca = sem("d_ca")
    d_x = Chan(sem, "d_x");   d_w = Chan(sem, "d_w", 4)
    d_at = Chan(sem, "d_at"); d_ao = Chan(sem, "d_ao")
    d_gt = Chan(sem, "d_gt"); d_gd = Chan(sem, "d_gd")
    d_gl = Chan(sem, "d_gl"); d_fo = Chan(sem, "d_fo")
    d_x2 = Chan(sem, "d_x2"); d_fi = Chan(sem, "d_fi")
    d_ou = Chan(sem, "d_ou"); d_p5 = Chan(sem, "d_p5")
    pe_t = sem("pe_t"); pe_k = sem("pe_k"); pe_s = sem("pe_s"); pe_v = sem("pe_v")
    pe_g = sem("pe_g"); pe_f = sem("pe_f")
    ac_q = sem("ac_q"); ac_p = sem("ac_p"); ac_f = sem("ac_f")
    ve_a = sem("ve_a"); ve_m = sem("ve_m"); ve_f = sem("ve_f"); ve_e = sem("ve_e")
    ve_o = sem("ve_o"); ve_n = sem("ve_n")

    def sb(name, shape, dtype):
        return ctx.enter_context(nc.sbuf_tensor(name, shape, dtype))

    import contextlib
    sel_sb = sb("sel_sb", [1, 1], I32)
    idbf_sb = sb("idbf_sb", [128, 128], BF16)
    idf_sb = sb("idf_sb", [128, 128], F32)
    stat = sb("stat", [128, 104], F32)
    fA = sb("fA", [128, 2 * E], F32)
    ntok = sb("ntok", [128, 2 * E], BF16)
    web = sb("web", [128, 4 * 8192], BF16)

    es1 = contextlib.ExitStack()

    def sb1(name, shape, dtype):
        return es1.enter_context(nc.sbuf_tensor(name, shape, dtype))

    cosq_sb = sb1("cosq_sb", [128, NT * 64], F32)
    sinq_sb = sb1("sinq_sb", [128, NT * 64], F32)
    cosk_sb = sb1("cosk_sb", [128, NT * 64], F32)
    sink_sb = sb1("sink_sb", [128, NT * 64], F32)
    headf = sb1("headf", [128, E], F32)
    headb = sb1("headb", [128, 2 * E], BF16)
    ropet = sb1("ropet", [128, E], F32)
    big_a = sb1("big_a", [128, NFC * TPC], BF16)

    psS = [ctx.enter_context(nc.psum_tensor(f"psS{i}", [128, 1024], F32))
           for i in range(2)]
    psB = [ctx.enter_context(nc.psum_tensor(f"psB{i}", [128, 512], F32))
           for i in range(2)]
    pt_bf = [ctx.enter_context(nc.psum_tensor(f"ptbf{i}", [128, 128], BF16))
             for i in range(2)]
    bk = [psS[0][:, 0:512], psS[0][:, 512:1024],
          psS[1][:, 0:512], psS[1][:, 512:1024]]

    kw = {}

    for dst, src in ((sel_sb, sel_in), (idbf_sb, idbf_in), (idf_sb, idf_in),
                     (cosq_sb, cosq_in), (sinq_sb, sinq_in),
                     (cosk_sb, cosk_in), (sink_sb, sink_in)):
        SY.dma_start(dst[:, :], src).then_inc(d_c, 16)
    kw["consts"] = d_c.n

    def ln_chain(base, src, scratch, out_ap):
        cS, cM, cQ, cV, cD, cR = (base + i for i in range(6))
        V.reduce_sum(stat[:, cS:cS + 1], src, axis=AX.X).then_inc(ve_a)
        V.wait(ve_a, ve_a.n)
        V.tensor_scalar_mul(stat[:, cM:cM + 1], stat[:, cS:cS + 1],
                            -1.0 / E).then_inc(ve_a)
        A.wait(ve_a, ve_a.n)
        A.activation(scratch, src, AF.Square, bias=stat[:, cM:cM + 1],
                     accum_out=stat[:, cQ:cQ + 1]).then_inc(ac_q)
        V.wait(ac_q, ac_q.n)
        V.tensor_scalar(stat[:, cV:cV + 1], stat[:, cQ:cQ + 1],
                        1.0 / E, 1e-5, ALU.mult, ALU.add).then_inc(ve_a)
        A.wait(ve_a, ve_a.n)
        A.activation(stat[:, cD:cD + 1], stat[:, cV:cV + 1], AF.Sqrt).then_inc(ac_q)
        V.wait(ac_q, ac_q.n)
        V.reciprocal(stat[:, cR:cR + 1], stat[:, cD:cD + 1]).then_inc(ve_a)
        V.wait(ve_a, ve_a.n)
        V.tensor_scalar(out_ap, src, stat[:, cM:cM + 1], stat[:, cR:cR + 1],
                        ALU.add, ALU.mult).then_inc(ve_a)
        return ve_a.n

    tp = {"u": 0, "evac": []}

    apply_done = []
    tr_done = []
    for tt in range(NT):
        if tt >= 2:
            SY.wait(ve_a, apply_done[tt - 2])
        SY.dma_start(fA[:, (tt % 2) * E:(tt % 2) * E + E],
                     x_in[tt * 128:tt * 128 + 128, :]).then_inc(d_x[tt], 16)
        V.wait(d_x[tt], d_x[tt].n)
        A.wait(d_x[tt], d_x[tt].n)
        if tt >= 2:
            V.wait(pe_t, tr_done[tt - 2])
            A.wait(pe_t, tr_done[tt - 2])
        nsl = ntok[:, (tt % 2) * E:(tt % 2) * E + E]
        ad = ln_chain((tt % 2) * 8, fA[:, (tt % 2) * E:(tt % 2) * E + E], nsl, nsl)
        apply_done.append(ad)
        T.wait(ve_a, ad)
        for k in range(NFC):
            u = tp["u"]
            if u >= 2:
                T.wait(ve_a, tp["evac"][u - 2])
            T.transpose(pt_bf[u % 2][:, :],
                        ntok[:, (tt % 2) * E + k * 128:(tt % 2) * E + k * 128 + 128],
                        idbf_sb[:, :]).then_inc(pe_t)
            V.wait(pe_t, pe_t.n)
            V.tensor_copy(big_a[:, k * TPC + tt * 128:k * TPC + tt * 128 + 128],
                          pt_bf[u % 2][:, :]).then_inc(ve_a)
            tp["evac"].append(ve_a.n)
            tp["u"] += 1
        tr_done.append(pe_t.n)
    kw["n1"] = ve_a.n

    p2ev = []
    p2st = []

    def load_eb_blocks(w_src, free_at):
        at = []
        for eb in range(4):
            G.wait(pe_k, free_at)
            G.dma_start(
                bass.AP(web, eb * 8192, [[4 * 8192, 128], [512, NFC], [1, 512]]),
                bass.AP(w_src.tensor, eb * 512, [[E, 128], [128 * E, NFC], [1, 512]])
            ).then_inc(d_w[eb], 16)
            at.append(d_w[eb].n)
        return at

    def proj_tok(w_src, dst_dram, cos_sb, sin_sb):
        w_at = load_eb_blocks(w_src, pe_k.n)
        for tt in range(NT):
            hb = headb[:, (tt % 2) * E:(tt % 2) * E + E]
            dst_row = headf if cos_sb is not None else hb
            for eb in range(4):
                u = len(p2ev)
                T.wait(d_w[eb], w_at[eb])
                if u == 0:
                    T.wait(ve_a, kw["n1"])
                if u >= 2:
                    T.wait(ve_o, p2ev[u - 2])
                for fc in range(NFC):
                    rec = T.matmul(
                        psB[u % 2][:, :],
                        big_a[:, fc * TPC + tt * 128:fc * TPC + tt * 128 + 128],
                        web[:, eb * 8192 + fc * 512:eb * 8192 + fc * 512 + 512],
                        start=(fc == 0), stop=(fc == NFC - 1))
                rec.then_inc(pe_k)
                V.wait(pe_k, pe_k.n)
                if eb == 0 and len(p2st) >= 2:
                    V.wait(d_st[tt], p2st[-2])
                V.tensor_copy(dst_row[:, eb * 512:eb * 512 + 512],
                              psB[u % 2][:, :]).then_inc(ve_o)
                p2ev.append(ve_o.n)
            if cos_sb is not None:
                V.wait(ve_o, p2ev[-1])
                h1 = bass.AP(headf, 0, [[E, 128], [128, H], [1, 64]])
                h2 = bass.AP(headf, 64, [[E, 128], [128, H], [1, 64]])
                cosv = bass.AP(cos_sb, tt * 64, [[NT * 64, 128], [0, H], [1, 64]])
                sinv = bass.AP(sin_sb, tt * 64, [[NT * 64, 128], [0, H], [1, 64]])
                t1 = bass.AP(ropet, 0, [[E, 128], [64, H], [1, 64]])
                t2 = bass.AP(ropet, 1024, [[E, 128], [64, H], [1, 64]])
                top = bass.AP(headb, (tt % 2) * E, [[2 * E, 128], [128, H], [1, 64]])
                bot = bass.AP(headb, (tt % 2) * E + 64, [[2 * E, 128], [128, H], [1, 64]])
                V.tensor_tensor(t1, h1, cosv, ALU.mult)
                V.tensor_tensor(t2, h2, sinv, ALU.mult)
                V.tensor_tensor(top, t1, t2, ALU.subtract)
                V.tensor_tensor(t1, h1, sinv, ALU.mult)
                V.tensor_tensor(t2, h2, cosv, ALU.mult)
                V.tensor_tensor(bot, t1, t2, ALU.add).then_inc(ve_a)
                G.wait(ve_a, ve_a.n)
            else:
                G.wait(ve_o, p2ev[-1])
            G.dma_start(dst_dram.ap()[tt * 128:tt * 128 + 128, :],
                        hb).then_inc(d_st[tt], 16)
            p2st.append(d_st[tt].n)

    proj_tok(win["wk"], k_con, cosk_sb, sink_sb)
    proj_tok(win["wv"], v_con, None, None)

    G.wait(d_st[0], d_st[0].n)
    G.wait(d_st[1], d_st[1].n)
    import os
    if not os.environ.get("MK_NOCC"):
        G.collective_compute(
            "AllGather", ALU.bypass, replica_groups=[[0, 1, 2, 3], [4, 5, 6, 7]],
            ins=[k_con.ap().opt()], outs=[g_k.ap().opt()]).then_inc(d_cc, 1)
        G.collective_compute(
            "AllGather", ALU.bypass, replica_groups=[[0, 1, 2, 3], [4, 5, 6, 7]],
            ins=[v_con.ap().opt()], outs=[g_v.ap().opt()]).then_inc(d_cc, 1)
    kw["cc"] = d_cc.n

    proj_tok(win["wq"], q_scr, cosq_sb, sinq_sb)
    kw["q_done"] = (d_st[0].n, d_st[1].n)
    kw["veo_p2"] = ve_o.n
    kw["p2_mm"] = pe_k.n

    for dst, src in ((wg_bf, win["wg"]), (wu_bf, win["wu"]), (wd_bf, win["wd"])):
        rows = dst.shape[0] // 4
        for p4 in range(4):
            G.dma_start(dst.ap()[p4 * rows:(p4 + 1) * rows, :],
                        src[p4 * rows:(p4 + 1) * rows, :]).then_inc(d_ca, 16)
    kw["cast"] = d_ca.n

    kw["wo_at"] = load_eb_blocks(win["wo"], pe_k.n)

    if dbg and stage == 1:
        G.wait(d_st[0], kw["q_done"][0])
        G.wait(d_st[1], kw["q_done"][1])
        for nm, src in (("d_kc", k_con), ("d_vc", v_con), ("d_q", q_scr)):
            G.dma_start(dbg_t[nm].ap()[:, :], src.ap()[:, :]).then_inc(d_ou[0], 16)
        G.dma_start(dbg_t["d_n1"].ap()[:, :], big_a[:, :]).then_inc(d_ou[0], 16)
        G.dma_start(dbg_t["d_nt"].ap()[:, :], ntok[:, :]).then_inc(d_ou[0], 16)
        G.wait(d_ou[0], d_ou[0].n)
        _finish(nc, P)
        return nc

    es1.close()
    esA = contextlib.ExitStack()

    def sbA(name, shape, dtype):
        return esA.enter_context(nc.sbuf_tensor(name, shape, dtype))

    mask_sb = sbA("mask_sb", [128, 896], F32)
    kbuf = sbA("kbuf", [128, 2 * S], BF16)
    vbuf = sbA("vbuf", [128, 2 * 8256], BF16)
    qbuf = sbA("qbuf", [128, 2 * TPC], BF16)
    p_sb = sbA("p_sb", [128, 2 * 1024], BF16)
    aostg = sbA("aostg", [128, 2 * 512], BF16)

    SY.wait(d_st[0], kw["q_done"][0])
    SY.wait(d_st[1], kw["q_done"][1])
    SY.dma_start(mask_sb[:, :], mask_in).then_inc(d_c, 16)
    kw["mask_at"] = d_c.n

    snap = [s.n for s in all_sems]
    var_streams = {e: [] for e in ("tensor", "vector", "scalar", "sync")}
    shared_g = None
    finals = None
    for j in range(4):
        for s, v in zip(all_sems, snap):
            s.n = v
        saved = P.streams
        P.streams = {e: [] for e in ENGINES}
        _attention(j, P, T, V, A, SY, G, kw, kbuf, vbuf, qbuf, p_sb, aostg,
                   mask_sb, stat, g_k, g_v, q_scr, ao_scr, psS, psB,
                   d_at, d_ao, d_cc, d_st, d_c, pe_s, pe_v, ac_p, ve_m, ve_f, ve_o)
        for e in var_streams:
            var_streams[e].append(P.streams[e])
        if shared_g is None:
            shared_g = P.streams["gpsimd"]
        f = [s.n for s in all_sems]
        if finals is None:
            finals = f
        else:
            assert f == finals, f"variant {j} sem totals differ"
        P.streams = saved
    for s, v in zip(all_sems, finals):
        s.n = v
    for e in var_streams:
        P.streams[e].append(VarRec("selr_" + e, sel_sb[0:1, 0:1], var_streams[e]))
    P.streams["gpsimd"].extend(shared_g)
    kw["att_ao"] = (d_ao[0].n, d_ao[1].n)
    kw["att_pe"] = pe_v.n
    kw["att_acp"] = ac_p.n

    if dbg and stage == 2:
        G.wait(d_ao[0], kw["att_ao"][0])
        G.wait(d_ao[1], kw["att_ao"][1])
        G.dma_start(dbg_t["d_ao"].ap()[:, :], ao_scr.ap()[:, :]).then_inc(d_ou[0], 16)
        G.wait(d_ou[0], d_ou[0].n)
        _finish(nc, P)
        return nc

    esA.close()
    es2 = contextlib.ExitStack()
    big_a2 = es2.enter_context(nc.sbuf_tensor("big_a2", [128, NFC * TPC], BF16))
    wcol = es2.enter_context(nc.sbuf_tensor("wcol", [128, 2 * 2048], BF16))
    fB = es2.enter_context(nc.sbuf_tensor("fB", [128, 2 * E], F32))
    es3 = contextlib.ExitStack()
    orow = es3.enter_context(nc.sbuf_tensor("orow", [128, E], F32))
    aotile = es3.enter_context(nc.sbuf_tensor("aotile", [128, 2 * E], BF16))
    big_a = big_a2

    op_ev = []
    x2st = []
    n2tr = []
    p5x = []
    for tt in range(NT):
        if tt == 0:
            SY.wait(d_ao[0], kw["att_ao"][0])
            SY.wait(d_ao[1], kw["att_ao"][1])
        if tt >= 2:
            SY.wait(pe_t, n2tr[tt - 2] if False else kw[f"p5_ao_used{tt - 2}"])
        SY.dma_start(aotile[:, (tt % 2) * E:(tt % 2) * E + E],
                     ao_scr.ap()[tt * 128:tt * 128 + 128, :]).then_inc(d_p5[tt], 16)
        ao_at = d_p5[tt].n
        if tt >= 2:
            SY.wait(ve_a, p5x[tt - 2])
        SY.dma_start(fA[:, (tt % 2) * E:(tt % 2) * E + E],
                     x_in[tt * 128:tt * 128 + 128, :]).then_inc(d_x[tt], 16)
        x_at = d_x[tt].n

        T.wait(d_p5[tt], ao_at)
        if tt == 0:
            T.wait(ve_o, kw["veo_p2"])
        for k in range(NFC):
            u = tp["u"]
            if u >= 2:
                T.wait(ve_a, tp["evac"][u - 2])
            T.transpose(pt_bf[u % 2][:, :],
                        aotile[:, (tt % 2) * E + k * 128:(tt % 2) * E + k * 128 + 128],
                        idbf_sb[:, :]).then_inc(pe_t)
            V.wait(pe_t, pe_t.n)
            V.tensor_copy(wcol[:, (tt % 2) * 2048 + k * 128:(tt % 2) * 2048 + k * 128 + 128],
                          pt_bf[u % 2][:, :]).then_inc(ve_a)
            tp["evac"].append(ve_a.n)
            tp["u"] += 1
        aofm_ready = ve_a.n

        T.wait(ve_a, aofm_ready)
        if tt == 0:
            for eb in range(4):
                T.wait(d_w[eb], kw["wo_at"][eb])
        for eb in range(4):
            u = tt * 4 + eb
            if u >= 2:
                T.wait(ve_o, op_ev[u - 2])
            for fc in range(NFC):
                rec = T.matmul(
                    psB[u % 2][:, :],
                    wcol[:, (tt % 2) * 2048 + fc * 128:(tt % 2) * 2048 + fc * 128 + 128],
                    web[:, eb * 8192 + fc * 512:eb * 8192 + fc * 512 + 512],
                    start=(fc == 0), stop=(fc == NFC - 1))
            rec.then_inc(pe_k)
            V.wait(pe_k, pe_k.n)
            V.tensor_copy(orow[:, eb * 512:eb * 512 + 512], psB[u % 2][:, :]).then_inc(ve_o)
            op_ev.append(ve_o.n)
        kw[f"p5_ao_used{tt}"] = pe_t.n

        V.wait(ve_o, op_ev[-1])
        nsl = ntok[:, (tt % 2) * E:(tt % 2) * E + E]
        x2sl = fB[:, (tt % 2) * E:(tt % 2) * E + E]
        if tt >= 2:
            V.wait(pe_t, n2tr[tt - 2])
            A.wait(pe_t, n2tr[tt - 2])
            V.wait(d_x2[tt], x2st[tt - 2])
        V.wait(d_x[tt], x_at)
        lo = ln_chain(16 + (tt % 2) * 8, orow[:, :], nsl, x2sl)
        V.tensor_tensor(x2sl, x2sl, fA[:, (tt % 2) * E:(tt % 2) * E + E],
                        ALU.add).then_inc(ve_a)
        p5x.append(ve_a.n)
        G.wait(ve_a, ve_a.n)
        if tt >= 2:
            pass
        G.dma_start(x2_scr.ap()[tt * 128:tt * 128 + 128, :], x2sl).then_inc(d_x2[tt], 16)
        x2st.append(d_x2[tt].n)
        if dbg and stage == 3:
            G.dma_start(dbg_t["d_x2"].ap()[tt * 128:tt * 128 + 128, :],
                        x2sl).then_inc(d_x2[tt], 16)
            x2st[-1] = d_x2[tt].n

        n2a = ln_chain(32 + (tt % 2) * 8, x2sl, nsl, nsl)
        if tt >= 1:
            pass
        T.wait(ve_a, n2a)
        for k in range(NFC):
            u = tp["u"]
            if u >= 2:
                T.wait(ve_a, tp["evac"][u - 2])
            T.transpose(pt_bf[u % 2][:, :],
                        ntok[:, (tt % 2) * E + k * 128:(tt % 2) * E + k * 128 + 128],
                        idbf_sb[:, :]).then_inc(pe_t)
            V.wait(pe_t, pe_t.n)
            V.tensor_copy(big_a[:, k * TPC + tt * 128:k * TPC + tt * 128 + 128],
                          pt_bf[u % 2][:, :]).then_inc(ve_a)
            tp["evac"].append(ve_a.n)
            tp["u"] += 1
        n2tr.append(pe_t.n)
        if tt >= 2 - 2:
            pass
    kw["n2"] = ve_a.n
    kw["p5_mm"] = pe_k.n
    kw["x2_all"] = (d_x2[0].n, d_x2[1].n)

    if dbg and stage == 3:
        G.wait(d_x2[0], kw["x2_all"][0])
        G.wait(d_x2[1], kw["x2_all"][1])
        _finish(nc, P)
        return nc

    es3.close()
    es4 = contextlib.ExitStack()
    sg_sb = es4.enter_context(nc.sbuf_tensor("sg_sb", [128, 2 * 512], F32))
    gd_sb = es4.enter_context(nc.sbuf_tensor("gd_sb", [128, 2 * 512], BF16))
    gtile = es4.enter_context(nc.sbuf_tensor("gtile", [128, 2 * 4096], BF16))
    ffosb = es4.enter_context(nc.sbuf_tensor("ffosb", [128, 2 * 512], F32))
    ffT = es4.enter_context(nc.sbuf_tensor("ffT", [128, 2 * 512], F32))

    gu_mm = []
    gu_ev = []
    gd_st = []
    sg_done = []
    for hcg in range(16):
        if hcg == 0:
            SY.wait(d_ca, kw["cast"])
            SY.wait(pe_k, kw["p5_mm"])
        if hcg >= 2:
            SY.wait(pe_g, gu_mm[hcg - 2])
        for fc in range(NFC):
            SY.dma_start(web[:, (hcg % 2) * 8192 + fc * 512:(hcg % 2) * 8192 + fc * 512 + 512],
                         wg_bf.ap()[hcg * 512:hcg * 512 + 512, fc * 128:fc * 128 + 128],
                         transpose=True).then_inc(d_gt[hcg], 16)
            SY.dma_start(web[:, (2 + hcg % 2) * 8192 + fc * 512:(2 + hcg % 2) * 8192 + fc * 512 + 512],
                         wu_bf.ap()[hcg * 512:hcg * 512 + 512, fc * 128:fc * 128 + 128],
                         transpose=True).then_inc(d_gt[hcg], 16)
        strips_at = d_gt[hcg].n

        T.wait(d_gt[hcg], strips_at)
        if hcg == 0:
            T.wait(ve_a, kw["n2"])
            T.wait(ve_o, ve_o.n)
        for hc in range(4):
            for tb in range(4):
                u = (hcg * 4 + hc) * 4 + tb
                if u >= 2:
                    T.wait(ve_e, gu_ev[u - 2])
                for fc in range(NFC):
                    recg = T.matmul(
                        bk[(u % 2) * 2],
                        web[:, (hcg % 2) * 8192 + fc * 512 + hc * 128:
                            (hcg % 2) * 8192 + fc * 512 + hc * 128 + 128],
                        big_a[:, fc * TPC + tb * 512:fc * TPC + tb * 512 + 512],
                        start=(fc == 0), stop=(fc == NFC - 1))
                recg.then_inc(pe_g)
                for fc in range(NFC):
                    recu = T.matmul(
                        bk[(u % 2) * 2 + 1],
                        web[:, (2 + hcg % 2) * 8192 + fc * 512 + hc * 128:
                            (2 + hcg % 2) * 8192 + fc * 512 + hc * 128 + 128],
                        big_a[:, fc * TPC + tb * 512:fc * TPC + tb * 512 + 512],
                        start=(fc == 0), stop=(fc == NFC - 1))
                recu.then_inc(pe_g)
                mm_at = pe_g.n
                A.wait(pe_g, mm_at - 1)
                if u >= 2:
                    A.wait(ve_e, gu_ev[u - 2])
                A.activation(sg_sb[:, (u % 2) * 512:(u % 2) * 512 + 512],
                             bk[(u % 2) * 2], AF.Silu).then_inc(ac_f)
                sg_done.append(ac_f.n)
                V.wait(ac_f, ac_f.n)
                V.wait(pe_g, mm_at)
                if u >= 2:
                    V.wait(d_gd[u], gd_st[u - 2])
                V.tensor_tensor(gd_sb[:, (u % 2) * 512:(u % 2) * 512 + 512],
                                sg_sb[:, (u % 2) * 512:(u % 2) * 512 + 512],
                                bk[(u % 2) * 2 + 1], ALU.mult).then_inc(ve_e)
                gu_ev.append(ve_e.n)
                G.wait(ve_e, ve_e.n)
                G.dma_start(gat_scr.ap()[(hcg * 4 + hc) * 128:(hcg * 4 + hc) * 128 + 128,
                                         tb * 512:tb * 512 + 512],
                            gd_sb[:, (u % 2) * 512:(u % 2) * 512 + 512]).then_inc(d_gd[u], 16)
                gd_st.append(d_gd[u].n)
        gu_mm.append(pe_g.n)
    kw["gat_all"] = (d_gd[0].n, d_gd[1].n)

    wd_at = {}
    dn_hc_mm = {}
    fo_st = []
    for ecg in range(4):
        for hc in range(64):
            if ecg == 0 and hc == 0:
                SY.wait(pe_g, pe_g.n)
                SY.wait(d_gd[0], kw["gat_all"][0])
                SY.wait(d_gd[1], kw["gat_all"][1])
            if ecg > 0:
                SY.wait(pe_g, dn_hc_mm[(ecg - 1, hc)])
            SY.dma_start(big_a[:, hc * 512:hc * 512 + 512],
                         wd_bf.ap()[ecg * 512:ecg * 512 + 512, hc * 128:hc * 128 + 128],
                         transpose=True).then_inc(d_gt[ecg], 16)
            wd_at[(ecg, hc)] = d_gt[ecg].n
        for tb in range(4):
            ub = ecg * 4 + tb
            if ub == 0:
                gl_groups = [(e2, t2, h2) for e2 in range(4) for t2 in range(4)
                             for h2 in range(8)]
                dn_gl_used = []

                def emit_gl(gg):
                    e2, t2, h2 = gl_groups[gg]
                    SY.dma_start(
                        bass.AP(gtile, (gg % 2) * 4096,
                                [[2 * 4096, 128], [512, 8], [1, 512]]),
                        bass.AP(gat_scr, (h2 * 8 * 128) * TPC + t2 * 512,
                                [[TPC, 128], [128 * TPC, 8], [1, 512]])
                    ).then_inc(d_gl[gg], 16)
                    return d_gl[gg].n
                gl_at = {}
                gl_at[0] = emit_gl(0)
                gl_at[1] = emit_gl(1)
            for hc in range(64):
                ug = ub * 64 + hc
                hg, hi = hc // 8, hc % 8
                gg = ub * 8 + hg
                if hi == 0:
                    T.wait(d_gl[gg], gl_at[gg])
                if tb == 0 and hc == 0:
                    T.wait(d_gt[ecg], wd_at[(ecg, 63)])
                if hc == 0 and ub >= 1:
                    T.wait(ve_e, kw[f"dn_ev{ub - 1}"])
                for ec in range(4):
                    rec = T.matmul(
                        bk[ec],
                        big_a[:, hc * 512 + ec * 128:hc * 512 + ec * 128 + 128],
                        gtile[:, (gg % 2) * 4096 + hi * 512:
                              (gg % 2) * 4096 + hi * 512 + 512],
                        start=(hc == 0), stop=(hc == 63))
                rec.then_inc(pe_g)
                if hi == 7:
                    dn_gl_used.append(pe_g.n)
                    if gg + 2 < len(gl_groups):
                        SY.wait(pe_g, dn_gl_used[gg])
                        gl_at[gg + 2] = emit_gl(gg + 2)
                dn_hc_mm[(ecg, hc)] = pe_g.n
            mm_done = pe_g.n
            for ec in range(4):
                ue = ub * 4 + ec
                V.wait(pe_g, mm_done)
                if ue >= 2:
                    V.wait(pe_f, kw[f"dn_tr{ue - 2}"])
                    V.wait(d_fo[ue], fo_st[ue - 2])
                V.tensor_copy(ffosb[:, (ue % 2) * 512:(ue % 2) * 512 + 512],
                              bk[ec]).then_inc(ve_e)
                if ec == 3:
                    kw[f"dn_ev{ub}"] = ve_e.n
                T.wait(ve_e, ve_e.n)
                for ttl in range(4):
                    uf = ue * 4 + ttl
                    if uf >= 2:
                        T.wait(ve_n, kw[f"dn_fT{uf - 2}"])
                    T.transpose(psB[uf % 2][:, 0:128],
                                ffosb[:, (ue % 2) * 512 + ttl * 128:
                                      (ue % 2) * 512 + ttl * 128 + 128],
                                idf_sb[:, :]).then_inc(pe_f)
                    V.wait(pe_f, pe_f.n)
                    V.tensor_copy(ffT[:, (ue % 2) * 512 + ttl * 128:
                                      (ue % 2) * 512 + ttl * 128 + 128],
                                  psB[uf % 2][:, 0:128]).then_inc(ve_n)
                    kw[f"dn_fT{uf}"] = ve_n.n
                kw[f"dn_tr{ue}"] = pe_f.n
                G.wait(ve_n, ve_n.n)
                if ue >= 2:
                    pass
                G.dma_start(
                    bass.AP(ffo_scr, (tb * 512) * E + (ecg * 4 + ec) * 128,
                            [[E, 128], [128 * E, 4], [1, 128]]),
                    bass.AP(ffT, (ue % 2) * 512, [[2 * 512, 128], [128, 4], [1, 128]])
                ).then_inc(d_fo[ue], 16)
                fo_st.append(d_fo[ue].n)
    kw["ffo_all"] = (d_fo[0].n, d_fo[1].n)

    for tt in range(NT):
        if tt == 0:
            SY.wait(d_fo[0], kw["ffo_all"][0])
            SY.wait(d_fo[1], kw["ffo_all"][1])
            SY.wait(d_x2[0], kw["x2_all"][0])
            SY.wait(d_x2[1], kw["x2_all"][1])
        if tt >= 2:
            SY.wait(d_ou[tt], kw[f"p6s{tt - 2}"])
        SY.dma_start(fA[:, (tt % 2) * E:(tt % 2) * E + E],
                     ffo_scr.ap()[tt * 128:tt * 128 + 128, :]).then_inc(d_fi[tt], 16)
        SY.dma_start(fB[:, (tt % 2) * E:(tt % 2) * E + E],
                     x2_scr.ap()[tt * 128:tt * 128 + 128, :]).then_inc(d_fi[tt], 16)
        V.wait(d_fi[tt], d_fi[tt].n)
        A.wait(d_fi[tt], d_fi[tt].n)
        nsl = ntok[:, (tt % 2) * E:(tt % 2) * E + E]
        osl = fA[:, (tt % 2) * E:(tt % 2) * E + E]
        ln_chain(48 + (tt % 2) * 8, osl, nsl, osl)
        V.tensor_tensor(osl, osl, fB[:, (tt % 2) * E:(tt % 2) * E + E],
                        ALU.add).then_inc(ve_a)
        kw[f"p6c{tt}"] = ve_a.n
        G.wait(ve_a, ve_a.n)
        G.dma_start(out_t.ap()[tt * 128:tt * 128 + 128, :], osl).then_inc(d_ou[tt], 16)
        kw[f"p6s{tt}"] = d_ou[tt].n
    G.wait(d_ou[0], d_ou[0].n)
    G.wait(d_ou[1], d_ou[1].n)

    _finish(nc, P)
    return nc


def _finish(nc, P):
    with nc.Block() as block:
        P.flush(block)


def _attention(j, P, T, V, A, SY, G, kw, kbuf, vbuf, qbuf, p_sb, aostg,
               mask_sb, stat, g_k, g_v, q_scr, ao_scr, psS, psB,
               d_at, d_ao, d_cc, d_st, d_c, pe_s, pe_v, ac_p, ve_m, ve_f, ve_o):
    qs_list, La_list = qsb_limits(j)
    nch = 8 - j
    parts = []
    for c in range(nch):
        if c < 8 - nch:
            parts.append((c, 0, 512))
            parts.append((c, 512, 512))
        else:
            parts.append((c, 0, 1024))
    assert len(parts) == 8

    def srcpos(c):
        jp = min(c, 7 - c)
        return jp, (0 if c == jp else 1024)

    nkt_tot = nch * 8
    ac_b = ac_p.n
    pv_b = pe_v.n
    gseq = 0
    pe_end = {}
    ones_at = {}
    st_at = {}

    V.wait(d_c, kw["mask_at"])
    for h in range(H):
        hs = h % 2
        if h == 0:
            SY.wait(d_cc, kw["cc"])
            SY.wait(d_st[0], kw["q_done"][0])
            SY.wait(d_st[1], kw["q_done"][1])
        if h >= 2:
            SY.wait(pe_v, pe_end[h - 2])
        for (c, off, wdt) in parts:
            jp, coff = srcpos(c)
            r0 = jp * 2048 + coff + off
            SY.dma_start(
                bass.AP(kbuf, hs * S + c * 1024 + off, [[2 * S, 128], [1, wdt]]),
                g_k.ap()[r0:r0 + wdt, h * 128:h * 128 + 128],
                transpose=True,
            ).then_inc(d_at[h], 16)
        for (c, off, wdt) in parts:
            jp, coff = srcpos(c)
            nkt = wdt // 128
            SY.dma_start(
                bass.AP(vbuf, hs * 8256 + (c * 8 + off // 128) * 129,
                        [[2 * 8256, 128], [129, nkt], [1, 128]]),
                bass.AP(g_v, (jp * 2048 + coff + off) * E + h * 128,
                        [[E, 128], [128 * E, nkt], [1, 128]])
            ).then_inc(d_at[h], 16)
        SY.dma_start(
            bass.AP(qbuf, hs * TPC, [[2 * TPC, 128], [1, TPC]]),
            q_scr.ap()[:, h * 128:h * 128 + 128],
            transpose=True,
        ).then_inc(d_at[h], 16)
        loads_at = d_at[h].n
        V.wait(d_at[h], loads_at)
        V.memset(bass.AP(vbuf, hs * 8256 + 128, [[2 * 8256, 128], [129, nkt_tot]]),
                 1.0).then_inc(ve_m)
        ones_at[h] = ve_m.n

        for s4 in range(4):
            qs, La = qs_list[s4], La_list[s4]
            sseq = h * 4 + s4
            ng = La // 2
            for g in range(ng):
                T.wait(ac_p, ac_b + gseq - 1)
                if gseq == 0:
                    T.wait(ve_o, kw["veo_p2"])
                if g == 0:
                    T.wait(d_at[h], loads_at)
                band = []
                for half in range(2):
                    kt = 2 * g + half
                    T.matmul(psS[gseq % 2][:, half * 512:half * 512 + 512],
                             kbuf[:, hs * S + kt * 128:hs * S + kt * 128 + 128],
                             qbuf[:, hs * TPC + s4 * 512:hs * TPC + s4 * 512 + 512],
                             start=True, stop=True).then_inc(pe_s)
                    if kt >= La - 4:
                        band.append((half, kt, pe_s.n))
                for (half, kt, cnt) in band:
                    V.wait(pe_s, cnt)
                    moff = qs - 128 * kt + 384
                    V.tensor_tensor(psS[gseq % 2][:, half * 512:half * 512 + 512],
                                    psS[gseq % 2][:, half * 512:half * 512 + 512],
                                    mask_sb[:, moff:moff + 512], ALU.add).then_inc(ve_m)
                A.wait(pe_s, pe_s.n)
                if band:
                    A.wait(ve_m, ve_m.n)
                A.wait(pe_v, pv_b + gseq - 1)
                A.activation(p_sb[:, (gseq % 2) * 1024:(gseq % 2) * 1024 + 1024],
                             psS[gseq % 2][:, :], AF.Exp).then_inc(ac_p)
                T.wait(ac_p, ac_p.n)
                if g == 0:
                    T.wait(ve_m, ones_at[h])
                    if sseq >= 1:
                        T.wait(ve_f, 2 * sseq)
                for half in range(2):
                    kt = 2 * g + half
                    for qq in range(4):
                        b, r = qq // 2, qq % 2
                        rec = T.matmul(
                            psB[b][:, r * 129:r * 129 + 129],
                            p_sb[:, (gseq % 2) * 1024 + half * 512 + qq * 128:
                                 (gseq % 2) * 1024 + half * 512 + qq * 128 + 128],
                            vbuf[:, hs * 8256 + kt * 129:hs * 8256 + kt * 129 + 129],
                            start=(g == 0 and half == 0 and r == 0),
                            stop=(g == ng - 1 and half == 1))
                rec.then_inc(pe_v)
                gseq += 1
            V.wait(pe_v, pe_v.n)
            if sseq >= 2:
                V.wait(d_ao[sseq], st_at[sseq - 2])
            for qq in range(4):
                b, r = qq // 2, qq % 2
                rec = V.reciprocal(stat[:, 96 + qq:97 + qq],
                                   psB[b][:, r * 129 + 128:r * 129 + 129])
            rec.then_inc(ve_f)
            V.wait(ve_f, ve_f.n)
            for qq in range(4):
                b, r = qq // 2, qq % 2
                rec = V.tensor_scalar_mul(
                    aostg[:, (sseq % 2) * 512 + qq * 128:(sseq % 2) * 512 + qq * 128 + 128],
                    psB[b][:, r * 129:r * 129 + 128],
                    stat[:, 96 + qq:97 + qq])
            rec.then_inc(ve_f)
            G.wait(ve_f, ve_f.n)
            G.dma_start(
                bass.AP(ao_scr, (s4 * 4 * 128) * E + h * 128,
                        [[E, 128], [128 * E, 4], [1, 128]]),
                bass.AP(aostg, (sseq % 2) * 512, [[2 * 512, 128], [128, 4], [1, 128]])
            ).then_inc(d_ao[sseq], 16)
            st_at[sseq] = d_ao[sseq].n
        pe_end[h] = pe_v.n


def make_core_inputs(L0_attn_x, wq, wk, wv, wo, w_gate, w_up, w_down):
    x = np.asarray(L0_attn_x, dtype=np.float32)
    ws = {nm: np.ascontiguousarray(np.asarray(w, dtype=np.float32))
          for nm, w in (("wq", wq), ("wk", wk), ("wv", wv), ("wo", wo),
                        ("wg", w_gate), ("wu", w_up), ("wd", w_down))}
    inv = (10000.0 ** (-(np.arange(0, D, 2, dtype=np.float64)) / D))
    scale = 1.0 / np.sqrt(D)
    idbf = np.eye(128, dtype=BF)
    idf = np.eye(128, dtype=np.float32)
    kk = np.arange(128)[:, None]
    zz = (np.arange(896) - 384)[None, :]
    mask = np.where(kk > zz, np.float32(-1e9), np.float32(0.0)).astype(np.float32)

    in_maps = []
    metas = []
    for r in range(8):
        b, j = r // 4, r % 4
        pos = np.concatenate([np.arange(1024 * j, 1024 * j + 1024),
                              np.arange(1024 * (7 - j), 1024 * (8 - j))])
        ang = pos.astype(np.float64)[:, None] * inv[None, :]
        cos = np.cos(ang).reshape(16, 128, 64).transpose(1, 0, 2).reshape(128, 1024)
        sin = np.sin(ang).reshape(16, 128, 64).transpose(1, 0, 2).reshape(128, 1024)
        m = {"x_own": np.ascontiguousarray(x[b, pos, :]),
             "cosq": (cos * scale).astype(np.float32),
             "sinq": (sin * scale).astype(np.float32),
             "cosk": cos.astype(np.float32),
             "sink": sin.astype(np.float32),
             "mask": mask, "idbf": idbf, "idf": idf,
             "sel": np.array([[j]], dtype=np.int32)}
        m.update(ws)
        in_maps.append(m)
        metas.append((b, j, pos))
    return in_maps, metas


def assemble(results, metas):
    out = np.zeros((B, S, E), dtype=np.float32)
    for r, (b, j, pos) in enumerate(metas):
        out[b, pos, :] = results[r]["out"]
    return out


_NC_CACHE = {}


def kernel(L0_attn_x, L0_attn_wq, L0_attn_wk, L0_attn_wv, L0_attn_wo,
           L0_ffn_w_gate, L0_ffn_w_up, L0_ffn_w_down):
    in_maps, metas = make_core_inputs(L0_attn_x, L0_attn_wq, L0_attn_wk,
                                      L0_attn_wv, L0_attn_wo,
                                      L0_ffn_w_gate, L0_ffn_w_up, L0_ffn_w_down)
    nc = build_nc(stage=4, dbg=False)
    res = run_bass_kernel_spmd(nc, in_maps, core_ids=list(range(8)))
    return assemble(res.results, metas)
